# version 01
# baseline (speedup 1.0000x reference)
import numpy as np
from contextlib import ExitStack
import concourse.bass as bass
import concourse.mybir as mybir
from concourse.bass_utils import run_bass_kernel_spmd

F32 = mybir.dt.float32
BF16 = mybir.dt.bfloat16
I32 = mybir.dt.int32
AF = mybir.ActivationFunctionType
ALU = mybir.AluOpType
AX = mybir.AxisListType

T = 2048
D = 1024
H = 16
INNER = 4096
NTT = T // 128
NDC = D // 128
NIC = INNER // 128
MAGIC = 12582912.0
EPS_LN = 1e-5
EPS_QN = 1e-6
NELEM_W = D * INNER

SIGN_TABLE = {
    0: [(0, 1.0), (1, -1.0), (2, -1.0), (3, -1.0)],
    1: [(1, 1.0), (0, 1.0), (3, -1.0), (2, 1.0)],
    2: [(2, 1.0), (3, 1.0), (0, 1.0), (1, -1.0)],
    3: [(3, 1.0), (2, -1.0), (1, 1.0), (0, 1.0)],
}


class Ctr:
    def __init__(self, stack, nc, name, ctx=None):
        self.h = stack.enter_context(nc.semaphore(name))
        self.n = 0
        self.ctx = ctx

    def inc(self, inst, k=1):
        inst.then_inc(self.h, k)
        self.n += k
        if self.ctx is not None:
            self.ctx._note_inc(inst, self)
        return self.n


_SERIAL_OPS = frozenset((
    "tensor_copy", "tensor_tensor", "tensor_add", "tensor_sub", "tensor_mul",
    "tensor_max", "tensor_scalar", "tensor_single_scalar", "tensor_scalar_add",
    "tensor_scalar_sub", "tensor_scalar_mul", "tensor_scalar_max",
    "tensor_scalar_min", "tensor_relu", "tensor_reduce", "reduce_max",
    "reduce_sum", "scalar_tensor_tensor", "tensor_tensor_scan", "reciprocal",
    "memset", "memzero", "activation", "sign", "square", "sqrt", "mul",
    "copy", "add", "select", "copy_predicated", "transpose", "iota",
    "partition_broadcast", "partition_all_reduce", "engine_nop", "pool",
    "pool_avg", "pool_max", "bn_stats", "bn_aggr", "max", "stream_shuffle",
))


class SerialEng:
    def __init__(self, ctr):
        self._eng = None
        self._ctr = ctr
        self._pend = None
        self._sync = None

    def _rebind(self, eng):
        self._eng = eng
        return self

    def _user_inc(self, inst, ctr):
        if self._pend is inst:
            self._pend = None
            self._sync = (ctr.h, ctr.n)

    def _fence(self):
        if self._pend is not None:
            v = self._ctr.inc(self._pend)
            self._sync = (self._ctr.h, v)
            self._pend = None
        if self._sync is not None:
            self._eng.wait_ge(*self._sync)
            self._sync = None

    def __getattr__(self, name):
        f = getattr(self._eng, name)
        if name not in _SERIAL_OPS:
            return f

        def g(*a, **k):
            self._fence()
            inst = f(*a, **k)
            self._pend = inst
            return inst
        return g


class Ctx:
    def __init__(self, nc, block, stack):
        self.nc = nc
        self.block = block
        self.stack = stack
        for c in ("pe", "dv", "sc", "gp", "ld", "lw", "st",
                  "hdr", "fb", "qw", "lx0", "lx1",
                  "dvs", "scs", "gps"):
            setattr(self, c, Ctr(stack, nc, "s_" + c, ctx=self))
        self._proxies = {"vector": SerialEng(self.dvs),
                         "scalar": SerialEng(self.scs),
                         "gpsimd": SerialEng(self.gps)}

    def _note_inc(self, inst, ctr):
        for p in self._proxies.values():
            p._user_inc(inst, ctr)

    def sec(self, eng, fn):
        if eng in self._proxies:
            getattr(self.block, eng)(lambda e: fn(self._proxies[eng]._rebind(e)))
        else:
            getattr(self.block, eng)(fn)

    def marks(self):
        return {c: getattr(self, c).n for c in ("pe", "dv", "sc", "gp", "ld", "lw", "st",
                                                "hdr", "fb", "qw", "lx0", "lx1")}

    def wait_marks(self, eng, m, which=("pe", "dv", "sc", "gp")):
        for w in which:
            if m[w] > 0:
                eng.wait_ge(getattr(self, w).h, m[w])

    def barrier(self, engines=("sync", "vector", "scalar", "tensor", "gpsimd"),
                on=("pe", "dv", "sc", "gp", "ld", "lw", "hdr", "fb", "qw", "lx0", "lx1")):
        m = self.marks()
        for e in engines:
            self.sec(e, lambda eng, m=m: self.wait_marks(eng, m, on))


def sb(ctx, name, shape, dt, stack=None):
    return (stack or ctx.stack).enter_context(ctx.nc.sbuf_tensor(name, shape, dt))


def psum(ctx, name, shape, dt=F32, stack=None):
    return (stack or ctx.stack).enter_context(ctx.nc.psum_tensor(name, shape, dt))


def _tap(ctx, dst, src_ap):
    def f(sy):
        m = ctx.marks()
        ctx.wait_marks(sy, m, ("pe", "dv", "sc", "gp", "ld", "lw",
                               "hdr", "fb", "qw", "lx0", "lx1"))
        ctx.st.inc(sy.dma_start(out=dst, in_=src_ap), 16)
    ctx.sec("sync", f)


def build(n_heads=H, taps=(), phase_stop=None):
    nc = bass.Bass("TRN2", target_bir_lowering=False, detect_race_conditions=False)
    taps = set(taps)
    E = {}

    for nm, shape in (("x", [T, D]), ("qW", [4, 256, 256]), ("kW", [4, 256, 256]),
                      ("vW", [4, 256, 256]), ("oW", [4, 256, 256]),
                      ("qb", [4, 256]), ("kb", [4, 256]), ("vb", [4, 256]),
                      ("ob", [4, 256]), ("J", [D]), ("wgT", [D, INNER]),
                      ("wuT", [D, INNER]), ("wdT", [INNER, D])):
        E[nm + "_d"] = nc.dram_tensor(nm, shape, F32, kind="ExternalInput").ap()
    E["idb_d"] = nc.dram_tensor("idb", [128, 128], BF16, kind="ExternalInput").ap()
    E["idf_d"] = nc.dram_tensor("idf", [128, 128], F32, kind="ExternalInput").ap()
    E["y_d"] = nc.dram_tensor("y", [T, D], F32, kind="ExternalOutput").ap()

    E["fbs_d"] = nc.dram_tensor("fbs", [4, D], F32, kind="Internal").ap()
    E["tg_d"] = nc.dram_tensor("tern_g", [D, INNER], BF16, kind="Internal").ap()
    E["tu_d"] = nc.dram_tensor("tern_u", [D, INNER], BF16, kind="Internal").ap()
    E["td_d"] = nc.dram_tensor("tern_d", [INNER, D], BF16, kind="Internal").ap()
    E["h_d"] = nc.dram_tensor("h_sc", [T, INNER], F32, kind="Internal").ap()
    E["x1_d"] = nc.dram_tensor("x1_sc", [T, D], F32, kind="Internal").ap()

    tap_shapes = {
        "xT": ([128, NDC * T], BF16), "qT": ([128, NDC * T], BF16),
        "kT": ([128, NDC * T], BF16), "Vaug": ([128, NTT * 1040], BF16),
        "attnoutT": ([128, NDC * T], BF16), "x1": ([T, D], F32),
        "PT0": ([128, 8 * T], BF16), "cb": ([128, 1], F32),
        "fbT": ([128, 32], F32), "WTo": ([128, NDC * D], BF16),
        "terng": ([D, INNER], BF16), "ternd": ([INNER, D], BF16),
        "wsc": ([128, 3], F32), "hsc": ([T, INNER], F32),
        "wacc": ([128, 2], F32), "wrow": ([1, 128], F32), "ws11": ([1, 2], F32),
    }
    E["tap_d"] = {name: nc.dram_tensor("tap_" + name, *tap_shapes[name],
                                       kind="ExternalOutput").ap() for name in taps}
    E["taps"] = taps

    with ExitStack() as st0:
        block = st0.enter_context(nc.Block())
        ctx = Ctx(nc, block, st0)

        for nm, shape, dt in (
                ("ones128", [128, 128], F32), ("identb", [128, 128], BF16),
                ("identf", [128, 128], F32), ("cb128", [128, 1], F32),
                ("epsqn", [128, 1], F32), ("epsln", [128, 1], F32),
                ("fbT", [128, 32], F32), ("fbrows", [1, 4 * D], F32),
                ("c11", [1, 1], F32), ("wsc128", [128, 3], F32),
                ("swq128", [128, 3], F32), ("dqa", [128, NTT], F32),
                ("sqa", [128, NTT], F32), ("dqg", [128, NTT], F32),
                ("dqu", [128, NTT], F32), ("dqd", [128, NTT], F32),
                ("WT_o", [128, NDC * D], BF16),
                ("xT", [128, NDC * T], BF16)):
            E[nm] = sb(ctx, nm, shape, dt)
        E["attnoutT"] = E["xT"]

        stA = st0.enter_context(ExitStack())
        E["qT"] = sb(ctx, "qT", [128, NDC * T], BF16, stA)
        E["kT"] = sb(ctx, "kT", [128, NDC * T], BF16, stA)
        E["Vaug"] = sb(ctx, "Vaug", [128, NTT * 1040], BF16, stA)

        build_p0(ctx, E, st0)
        if phase_stop == "P0":
            finish(ctx); return nc
        build_wq(ctx, E, st0)
        if phase_stop == "WQ":
            finish(ctx); return nc
        build_proj(ctx, E, st0)
        if phase_stop == "PROJ":
            finish(ctx); return nc
        ctx.barrier()

        stT = st0.enter_context(ExitStack())
        build_attn(ctx, E, stT, n_heads)
        ctx.barrier()
        if phase_stop == "ATTN":
            finish(ctx); return nc
        stT.close()
        stA.close()

        stF = st0.enter_context(ExitStack())
        build_ffn(ctx, E, stF)
        finish(ctx)
    return nc


def finish(ctx):
    def f(sy):
        m = ctx.marks()
        ctx.wait_marks(sy, m, ("pe", "dv", "sc", "gp", "ld", "lw",
                               "hdr", "fb", "qw", "lx0", "lx1"))
        sy.wait_ge(ctx.st.h, ctx.st.n)
    ctx.sec("sync", f)


def build_p0(ctx, E, stX):
    taps, tap_d = E["taps"], E["tap_d"]
    ones128, identb, identf = E["ones128"], E["identb"], E["identf"]
    cb128, fbT, fbrows, c11 = E["cb128"], E["fbT"], E["fbrows"], E["c11"]
    Vaug, xT = E["Vaug"], E["xT"]

    with ExitStack() as s0:
        xld = sb(ctx, "xld", [128, 2 * D], F32, s0)
        xbf = sb(ctx, "xbf", [128, 2 * D], BF16, s0)
        fb32 = sb(ctx, "fb32", [32, 128], F32, s0)
        brow = sb(ctx, "brow", [1, D], F32, s0)
        jrow = sb(ctx, "jrow", [1, D], F32, s0)
        pxt = psum(ctx, "pxt", [128, 2 * D], BF16, s0)
        pfb = psum(ctx, "pfb", [128, 32], F32, s0)
        pbc = psum(ctx, "pbc", [128, 1], F32, s0)

        cast_mark, evacT_mark = {}, {}
        n = ctx.dv.n + 2
        for tt in range(NTT):
            n += 1
            cast_mark[tt] = n
            if tt >= 1:
                n += 1
                evacT_mark[tt - 1] = n
        n += 1
        evacT_mark[NTT - 1] = n
        cb_mark, fbT_mark = n + 1, n + 2
        pe0 = ctx.pe.n
        xpose_mark = {tt: pe0 + tt + 1 for tt in range(NTT)}
        cbc_mark, pfb_mark = pe0 + NTT + 1, pe0 + NTT + 2
        HDRN = 7 * 16

        def xwait(eng, tt):
            lx = ctx.lx0 if tt % 2 == 0 else ctx.lx1
            eng.wait_ge(lx.h, 16 * (tt // 2 + 1))

        def sy1(sy):
            ctx.hdr.inc(sy.dma_start(out=jrow[:], in_=E["J_d"].rearrange("(p n) -> p n", p=1)), 16)
            for P, nm in enumerate(("qb", "kb", "vb", "ob")):
                ctx.hdr.inc(sy.dma_start(out=fbrows[:, P * D:(P + 1) * D],
                                         in_=E[nm + "_d"].rearrange("(o a) b -> o (a b)", o=1)), 16)
            ctx.hdr.inc(sy.dma_start(out=identb[:], in_=E["idb_d"]), 16)
            ctx.hdr.inc(sy.dma_start(out=identf[:], in_=E["idf_d"]), 16)
            for tt in range(NTT):
                if tt >= 2:
                    sy.wait_ge(ctx.dv.h, cast_mark[tt - 2])
                lx = ctx.lx0 if tt % 2 == 0 else ctx.lx1
                lx.inc(sy.dma_start(out=xld[:, (tt % 2) * D:(tt % 2 + 1) * D],
                                    in_=E["x_d"][tt * 128:(tt + 1) * 128, :]), 16)
        ctx.sec("sync", sy1)

        def sc1(s):
            s.wait_ge(ctx.hdr.h, HDRN)
            ctx.sc.inc(s.activation(out=jrow[:], in_=jrow[:], func=AF.Sin))
        ctx.sec("scalar", sc1)
        sc_sin = ctx.sc.n

        def ve1(v):
            v.memset(ones128[:], 1.0)
            v.memset(E["epsqn"][:], EPS_QN)
            v.memset(E["epsln"][:], EPS_LN)
            v.memset(Vaug[:], 1.0)
            v.wait_ge(ctx.sc.h, sc_sin)
            v.reduce_sum(out=c11[:], in_=jrow[:], axis=AX.X)
            got = ctx.dv.inc(v.tensor_scalar(out=c11[:], in0=c11[:], scalar1=0.1 / D,
                                             scalar2=-8.0, op0=ALU.mult, op1=ALU.add))
            assert got == 1
            v.wait_ge(ctx.hdr.h, HDRN)
            for P in range(4):
                base = P * D
                b = lambda w: fbrows[:, base + w * 256: base + (w + 1) * 256]
                t01, t01m = brow[:, 0:256], brow[:, 256:512]
                t23, t23m = brow[:, 512:768], brow[:, 768:1024]
                v.tensor_add(out=t01, in0=b(0), in1=b(1))
                v.tensor_sub(out=t01m, in0=b(0), in1=b(1))
                v.tensor_add(out=t23, in0=b(2), in1=b(3))
                v.tensor_sub(out=t23m, in0=b(2), in1=b(3))
                v.tensor_sub(out=b(0), in0=t01m, in1=t23)
                v.tensor_add(out=b(1), in0=t01, in1=t23m)
                v.tensor_add(out=b(2), in0=t01m, in1=t23)
                last = v.tensor_sub(out=b(3), in0=t01, in1=t23m)
            ctx.dv.inc(last)
            xTr = xT.ap().rearrange("p (dc t) -> p dc t", dc=NDC)
            pxtr = pxt.ap().rearrange("p (b dc t) -> p b dc t", b=2, dc=NDC)

            def evac(tt):
                v.wait_ge(ctx.pe.h, xpose_mark[tt])
                got = ctx.dv.inc(v.tensor_copy(out=xTr[:, :, tt * 128:(tt + 1) * 128],
                                               in_=pxtr[:, tt % 2, :, :]))
                assert got == evacT_mark[tt], (got, evacT_mark[tt])
            for tt in range(NTT):
                xwait(v, tt)
                got = ctx.dv.inc(v.tensor_copy(out=xbf[:, (tt % 2) * D:(tt % 2 + 1) * D],
                                               in_=xld[:, (tt % 2) * D:(tt % 2 + 1) * D]))
                assert got == cast_mark[tt], (got, cast_mark[tt])
                if tt >= 1:
                    evac(tt - 1)
            evac(NTT - 1)
            v.wait_ge(ctx.pe.h, cbc_mark)
            got = ctx.dv.inc(v.tensor_copy(out=cb128[:], in_=pbc[:]))
            assert got == cb_mark
            v.wait_ge(ctx.pe.h, pfb_mark)
            got = ctx.dv.inc(v.tensor_copy(out=fbT[:], in_=pfb[:]))
            assert got == fbT_mark
        ctx.sec("vector", ve1)

        def te1(t):
            for tt in range(NTT):
                t.wait_ge(ctx.dv.h, cast_mark[tt])
                if tt >= 2:
                    t.wait_ge(ctx.dv.h, evacT_mark[tt - 2])
                for dc in range(NDC):
                    mm = t.transpose(
                        out=pxt[:, (tt % 2) * D + dc * 128:(tt % 2) * D + (dc + 1) * 128],
                        in_=xbf[:, (tt % 2) * D + dc * 128:(tt % 2) * D + (dc + 1) * 128],
                        identity=identb[:])
                got = ctx.pe.inc(mm)
                assert got == xpose_mark[tt]
            t.wait_ge(ctx.dv.h, 1)
            got = ctx.pe.inc(t.matmul(out=pbc[:], lhsT=ones128[0:1, :], rhs=c11[:],
                                      start=True, stop=True))
            assert got == cbc_mark
            t.wait_ge(ctx.fb.h, 16)
            got = ctx.pe.inc(t.transpose(out=pfb[:], in_=fb32[:], identity=identf[0:32, 0:32]))
            assert got == pfb_mark
        ctx.sec("tensor", te1)

        def sy2(sy):
            sy.wait_ge(ctx.dv.h, 2)
            base = ctx.st.n
            for P in range(4):
                ctx.st.inc(sy.dma_start(out=E["fbs_d"][P:P + 1, :],
                                        in_=fbrows[:, P * D:(P + 1) * D]), 16)
            sy.wait_ge(ctx.st.h, base + 64)
            ctx.fb.inc(sy.dma_start(out=fb32[:],
                                    in_=E["fbs_d"].rearrange("a (c p) -> (a c) p", p=128)), 16)
        ctx.sec("sync", sy2)

        if "xT" in taps:
            _tap(ctx, tap_d["xT"], xT[:])
        if "cb" in taps:
            _tap(ctx, tap_d["cb"], cb128[:])
        if "fbT" in taps:
            _tap(ctx, tap_d["fbT"], fbT[:])
        ctx.barrier()


def build_wq(ctx, E, stX):
    taps, tap_d = E["taps"], E["tap_d"]
    wsc128, swq128, ones128, identf = E["wsc128"], E["swq128"], E["ones128"], E["identf"]

    with ExitStack() as s0:
        CW = 1024
        wbuf = sb(ctx, "wq_wbuf", [128, 2 * CW], F32, s0)
        t1 = sb(ctx, "wq_t1", [128, 2 * CW], F32, s0)
        tbuf = sb(ctx, "wq_tbuf", [128, 2 * CW], BF16, s0)
        accs = sb(ctx, "wq_acc", [128, 2], F32, s0)
        rowx = sb(ctx, "wq_row", [1, 128], F32, s0)
        s11 = sb(ctx, "wq_s11", [1, 2], F32, s0)
        pr = psum(ctx, "wq_pr", [1, 128], F32, s0)
        pb2 = psum(ctx, "wq_pb2", [128, 2], F32, s0)

        prev_q_done = 0
        for mi, (wk, tk_) in enumerate((("wgT_d", "tg_d"), ("wuT_d", "tu_d"),
                                        ("wdT_d", "td_d"))):
            w_d, t_d = E[wk], E[tk_]
            rows, cols = w_d.shape
            ncc = cols // CW if cols >= CW else 1
            cw = min(cols, CW)
            chunks = [(r, c) for r in range(rows // 128) for c in range(ncc)]
            nch = len(chunks)

            m0 = ctx.marks()
            p1_mark = {ch: m0["dv"] + ch + 1 for ch in range(nch)}
            abs_mark = {ch: m0["sc"] + ch + 1 for ch in range(nch)}

            def p1_sy(sy, chunks=chunks, w_d=w_d, cw=cw, m0=m0, abs_mark=abs_mark,
                      prev_q_done=prev_q_done):
                for ch, (r, c) in enumerate(chunks):
                    if ch >= 2:
                        sy.wait_ge(ctx.sc.h, abs_mark[ch - 2])
                    elif prev_q_done:
                        sy.wait_ge(ctx.dv.h, prev_q_done)
                    ctx.lw.inc(sy.dma_start(
                        out=wbuf[:, (ch % 2) * cw:(ch % 2 + 1) * cw],
                        in_=w_d[r * 128:(r + 1) * 128, c * cw:(c + 1) * cw]), 16)
            ctx.sec("sync", p1_sy)

            def p1_sc(s, nch=nch, cw=cw, m0=m0, p1_mark=p1_mark, abs_mark=abs_mark):
                for ch in range(nch):
                    s.wait_ge(ctx.lw.h, m0["lw"] + 16 * min(ch + 2, nch))
                    if ch >= 2:
                        s.wait_ge(ctx.dv.h, p1_mark[ch - 2])
                    got = ctx.sc.inc(s.activation(
                        out=t1[:, (ch % 2) * cw:(ch % 2 + 1) * cw],
                        in_=wbuf[:, (ch % 2) * cw:(ch % 2 + 1) * cw], func=AF.Abs))
                    assert got == abs_mark[ch]
            ctx.sec("scalar", p1_sc)

            def p1_ve(v, nch=nch, cw=cw, m0=m0, p1_mark=p1_mark, abs_mark=abs_mark):
                for ch in range(nch):
                    v.wait_ge(ctx.sc.h, abs_mark[ch])
                    v.reduce_sum(out=accs[:, 1:2],
                                 in_=t1[:, (ch % 2) * cw:(ch % 2 + 1) * cw], axis=AX.X)
                    if ch == 0:
                        got = ctx.dv.inc(v.tensor_copy(out=accs[:, 0:1], in_=accs[:, 1:2]))
                    else:
                        got = ctx.dv.inc(v.tensor_add(out=accs[:, 0:1], in0=accs[:, 0:1],
                                                      in1=accs[:, 1:2]))
                    assert got == p1_mark[ch]
            ctx.sec("vector", p1_ve)
            m1 = ctx.marks()

            def p1_te(t, m1=m1):
                t.wait_ge(ctx.dv.h, m1["dv"])
                ctx.pe.inc(t.transpose(out=pr[:], in_=accs[:, 0:1], identity=identf[:]))
            ctx.sec("tensor", p1_te)
            m2 = ctx.marks()

            def p1_ve2(v, m2=m2):
                v.wait_ge(ctx.pe.h, m2["pe"])
                v.tensor_copy(out=rowx[:], in_=pr[:])
                v.reduce_sum(out=s11[:, 0:1], in_=rowx[:], axis=AX.X)
                v.tensor_scalar(out=s11[:, 0:1], in0=s11[:, 0:1], scalar1=1.0 / NELEM_W,
                                scalar2=1e-5, op0=ALU.mult, op1=ALU.max)
                ctx.dv.inc(v.reciprocal(out=s11[:, 1:2], in_=s11[:, 0:1]))
            ctx.sec("vector", p1_ve2)
            m3 = ctx.marks()

            def p1_te2(t, m3=m3):
                t.wait_ge(ctx.dv.h, m3["dv"])
                t.matmul(out=pb2[:, 0:1], lhsT=ones128[0:1, :], rhs=s11[:, 0:1],
                         start=True, stop=True)
                ctx.pe.inc(t.matmul(out=pb2[:, 1:2], lhsT=ones128[0:1, :], rhs=s11[:, 1:2],
                                    start=True, stop=True))
            ctx.sec("tensor", p1_te2)
            m4 = ctx.marks()

            def p1_ve3(v, mi=mi, m4=m4):
                v.wait_ge(ctx.pe.h, m4["pe"])
                v.tensor_copy(out=wsc128[:, mi:mi + 1], in_=pb2[:, 0:1])
                ctx.dv.inc(v.tensor_copy(out=swq128[:, mi:mi + 1], in_=pb2[:, 1:2]))
            ctx.sec("vector", p1_ve3)
            m5 = ctx.marks()

            q_mark = {ch: m5["dv"] + ch + 1 for ch in range(nch)}
            s_mark = {ch: m5["st"] + 16 * (ch + 1) for ch in range(nch)}

            def p2_sy(sy, chunks=chunks, w_d=w_d, t_d=t_d, cw=cw, m5=m5,
                      q_mark=q_mark, abs_last=abs_mark[nch - 1]):
                for ch, (r, c) in enumerate(chunks):
                    if ch >= 2:
                        sy.wait_ge(ctx.dv.h, q_mark[ch - 2])
                    else:
                        sy.wait_ge(ctx.sc.h, abs_last)
                    ctx.lw.inc(sy.dma_start(
                        out=wbuf[:, (ch % 2) * cw:(ch % 2 + 1) * cw],
                        in_=w_d[r * 128:(r + 1) * 128, c * cw:(c + 1) * cw]), 16)
                    if ch >= 1:
                        r0, c0 = chunks[ch - 1]
                        sy.wait_ge(ctx.dv.h, q_mark[ch - 1])
                        ctx.st.inc(sy.dma_start(
                            out=t_d[r0 * 128:(r0 + 1) * 128, c0 * cw:(c0 + 1) * cw],
                            in_=tbuf[:, ((ch - 1) % 2) * cw:((ch - 1) % 2 + 1) * cw]), 16)
                r0, c0 = chunks[nch - 1]
                sy.wait_ge(ctx.dv.h, q_mark[nch - 1])
                ctx.st.inc(sy.dma_start(
                    out=t_d[r0 * 128:(r0 + 1) * 128, c0 * cw:(c0 + 1) * cw],
                    in_=tbuf[:, ((nch - 1) % 2) * cw:((nch - 1) % 2 + 1) * cw]), 16)
            ctx.sec("sync", p2_sy)

            def p2_ve(v, mi=mi, nch=nch, cw=cw, m5=m5, q_mark=q_mark, s_mark=s_mark):
                for ch in range(nch):
                    v.wait_ge(ctx.lw.h, m5["lw"] + 16 * min(ch + 2, nch))
                    if ch >= 2:
                        v.wait_ge(ctx.st.h, s_mark[ch - 2])
                    src = wbuf[:, (ch % 2) * cw:(ch % 2 + 1) * cw]
                    tc = t1[:, 0:cw]
                    v.tensor_scalar(out=tc, in0=src, scalar1=swq128[:, mi:mi + 1],
                                    scalar2=MAGIC, op0=ALU.mult, op1=ALU.add)
                    v.tensor_scalar(out=tc, in0=tc, scalar1=MAGIC, scalar2=1.0,
                                    op0=ALU.subtract, op1=ALU.min)
                    v.tensor_single_scalar(out=tc, in_=tc, scalar=-1.0, op=ALU.max)
                    v.tensor_copy(out=src.bitcast(I32), in_=tc.bitcast(I32))
                    got = ctx.dv.inc(v.tensor_copy(
                        out=tbuf[:, (ch % 2) * cw:(ch % 2 + 1) * cw], in_=src))
                    assert got == q_mark[ch]
            ctx.sec("vector", p2_ve)
            prev_q_done = q_mark[nch - 1]

        E["tern_st_mark"] = ctx.st.n
        if "terng" in taps or "ternd" in taps:
            def ftp(sy):
                m = ctx.marks()
                ctx.wait_marks(sy, m, ("pe", "dv", "sc", "gp", "ld", "lw"))
                sy.wait_ge(ctx.st.h, m["st"])
                if "terng" in taps:
                    ctx.st.inc(sy.dma_start(out=tap_d["terng"], in_=E["tg_d"]), 16)
                if "ternd" in taps:
                    ctx.st.inc(sy.dma_start(out=tap_d["ternd"], in_=E["td_d"]), 16)
            ctx.sec("sync", ftp)
        if "wsc" in taps:
            _tap(ctx, tap_d["wsc"], wsc128[:])
        if "wacc" in taps:
            _tap(ctx, tap_d["wacc"], accs[:])
        if "wrow" in taps:
            _tap(ctx, tap_d["wrow"], rowx[:])
        if "ws11" in taps:
            _tap(ctx, tap_d["ws11"], s11[:])
        ctx.barrier()


def build_proj(ctx, E, stX):
    taps, tap_d = E["taps"], E["tap_d"]
    xT, qT, kT, Vaug, WT_o = E["xT"], E["qT"], E["kT"], E["Vaug"], E["WT_o"]
    fbT, fbrows, ones128, epsqn = E["fbT"], E["fbrows"], E["ones128"], E["epsqn"]

    with ExitStack() as s0:
        qWs = sb(ctx, "qWs", [128, 4 * 256], F32, s0)
        WTx = sb(ctx, "WTx", [128, NDC * D], BF16, s0)
        sqs = sb(ctx, "sqs", [128, 4 * 512], F32, s0)
        nb = sb(ctx, "nb", [128, 512], F32, s0)
        invb = sb(ctx, "invb", [128, 512], F32, s0)
        vbbc = sqs.ap()[:, 0:D]
        pq = psum(ctx, "pq", [128, 4 * 512], F32, s0)
        pv = psum(ctx, "pv", [128, 2 * 512], F32, s0)
        pbb = psum(ctx, "pbb", [128, 512], F32, s0)

        for nm in ("oW", "qW", "kW", "vW"):
            pidx = {"qW": 0, "kW": 1, "vW": 2, "oW": 3}[nm]
            WT_dst = WT_o if nm == "oW" else WTx
            mld = ctx.marks()

            for wh in range(2):
                mld2 = ctx.marks()

                def ld_w(sy, w_d=E[nm + "_d"], wh=wh, mld=mld, mld2=mld2):
                    if wh == 0:
                        ctx.wait_marks(sy, mld, ("pe", "dv"))
                    else:
                        sy.wait_ge(ctx.dv.h, mld2["dv"])
                    ctx.qw.inc(sy.dma_start(
                        out=qWs.ap().rearrange("p (w dh o) -> p w dh o", w=2, dh=2),
                        in_=w_d.rearrange("w (dh p) o -> p w dh o",
                                          p=128)[:, wh * 2:(wh + 1) * 2]), 16)
                ctx.sec("sync", ld_w)
                mw = ctx.marks()

                def asm(v, WT_dst=WT_dst, wh=wh, mw=mw):
                    v.wait_ge(ctx.qw.h, mw["qw"])
                    if wh == 0:
                        v.wait_ge(ctx.pe.h, mw["pe"])
                    for dc in range(NDC):
                        c, dh = dc // 2, dc % 2
                        for p in range(4):
                            w, sign = SIGN_TABLE[p][c]
                            if w // 2 != wh:
                                continue
                            last = v.tensor_scalar_mul(
                                out=WT_dst[:, dc * D + p * 256:dc * D + (p + 1) * 256],
                                in0=qWs[:, ((w % 2) * 2 + dh) * 256:((w % 2) * 2 + dh + 1) * 256],
                                scalar1=sign)
                    ctx.dv.inc(last)
                ctx.sec("vector", asm)
            masm = ctx.marks()

            if nm == "oW":
                continue

            if nm in ("qW", "kW"):
                dstT = qT if nm == "qW" else kT
                sched = {}
                pe_n, sc_n, dv_n = ctx.pe.n, ctx.sc.n, ctx.dv.n
                for i, (g, ts_) in enumerate([(g, s_) for g in range(2) for s_ in range(4)]):
                    pe_n += 1
                    sc_n += 1
                    dv_n += 1
                    sc_n += 1
                    dv_n += 4
                    sched[(g, ts_)] = dict(mm=pe_n, sq=sc_n - 1, ssum=dv_n - 4,
                                           sqrt=sc_n, out=dv_n)

                def qk_te(t, masm=masm, sched=sched):
                    t.wait_ge(ctx.dv.h, masm["dv"])
                    for g in range(2):
                        for ts_ in range(4):
                            if (g, ts_) != (0, 0):
                                prev = (g, ts_ - 1) if ts_ > 0 else (g - 1, 3)
                                t.wait_ge(ctx.dv.h, sched[prev]["out"])
                            for jj in range(4):
                                j = g + 2 * jj
                                for dc in range(NDC):
                                    mm = t.matmul(
                                        out=pq[:, jj * 512:(jj + 1) * 512],
                                        lhsT=WTx[:, dc * D + j * 128:dc * D + (j + 1) * 128],
                                        rhs=xT[:, dc * T + ts_ * 512:dc * T + (ts_ + 1) * 512],
                                        start=(dc == 0), stop=(dc == NDC - 1))
                            got = ctx.pe.inc(mm)
                            assert got == sched[(g, ts_)]["mm"]
                ctx.sec("tensor", qk_te)

                def qk_sc(s, pidx=pidx, sched=sched):
                    for g in range(2):
                        for ts_ in range(4):
                            sd = sched[(g, ts_)]
                            s.wait_ge(ctx.pe.h, sd["mm"])
                            for jj in range(4):
                                j = g + 2 * jj
                                a = s.activation(out=sqs[:, jj * 512:(jj + 1) * 512],
                                                 in_=pq[:, jj * 512:(jj + 1) * 512],
                                                 func=AF.Square,
                                                 bias=fbT[:, pidx * 8 + j:pidx * 8 + j + 1])
                            got = ctx.sc.inc(a)
                            assert got == sd["sq"]
                            s.wait_ge(ctx.dv.h, sd["ssum"])
                            got = ctx.sc.inc(s.activation(out=nb[:], in_=sqs[:, 0:512],
                                                          func=AF.Sqrt, bias=epsqn[:]))
                            assert got == sd["sqrt"]
                ctx.sec("scalar", qk_sc)

                def qk_ve(v, pidx=pidx, dstT=dstT, sched=sched):
                    for g in range(2):
                        for ts_ in range(4):
                            sd = sched[(g, ts_)]
                            v.wait_ge(ctx.sc.h, sd["sq"])
                            v.tensor_add(out=sqs[:, 0:512], in0=sqs[:, 0:512],
                                         in1=sqs[:, 512:1024])
                            v.tensor_add(out=sqs[:, 0:512], in0=sqs[:, 0:512],
                                         in1=sqs[:, 1024:1536])
                            got = ctx.dv.inc(v.tensor_add(out=sqs[:, 0:512],
                                                          in0=sqs[:, 0:512],
                                                          in1=sqs[:, 1536:2048]))
                            assert got == sd["ssum"]
                            v.wait_ge(ctx.sc.h, sd["sqrt"])
                            v.reciprocal(out=invb[:], in_=nb[:])
                            for jj in range(4):
                                j = g + 2 * jj
                                got = ctx.dv.inc(v.scalar_tensor_tensor(
                                    out=dstT[:, j * T + ts_ * 512:j * T + (ts_ + 1) * 512],
                                    in0=pq[:, jj * 512:(jj + 1) * 512],
                                    scalar=fbT[:, pidx * 8 + j:pidx * 8 + j + 1],
                                    in1=invb[:], op0=ALU.add, op1=ALU.mult))
                            assert got == sd["out"]
                ctx.sec("vector", qk_ve)
            else:
                mv0 = ctx.marks()
                bb_pe = {0: mv0["pe"] + 1, 1: mv0["pe"] + 2}
                bb_dv = {0: mv0["dv"] + 1, 1: mv0["dv"] + 2}

                def vb_te(t, mv0=mv0, bb_dv=bb_dv):
                    t.wait_ge(ctx.dv.h, mv0["dv"])
                    for os in range(2):
                        if os == 1:
                            t.wait_ge(ctx.dv.h, bb_dv[0])
                        ctx.pe.inc(t.matmul(
                            out=pbb[:], lhsT=ones128[0:1, :],
                            rhs=fbrows[:, 2 * D + os * 512:2 * D + (os + 1) * 512],
                            start=True, stop=True))
                ctx.sec("tensor", vb_te)

                def vb_ve(v, bb_pe=bb_pe):
                    for os in range(2):
                        v.wait_ge(ctx.pe.h, bb_pe[os])
                        ctx.dv.inc(v.tensor_copy(out=vbbc[:, os * 512:(os + 1) * 512],
                                                 in_=pbb[:]))
                ctx.sec("vector", vb_ve)
                mvb = ctx.marks()

                Vr = Vaug.ap().rearrange("p (tt h c) -> p tt h c", tt=NTT, h=H)
                vb_r = vbbc.rearrange("p (os h c) -> p os h c", os=2, c=64)
                pv_r = pv.ap().rearrange("p (b h c) -> p b h c", b=2, c=64)
                vmm = {i: mvb["pe"] + i + 1 for i in range(32)}
                vev = {i: mvb["dv"] + i + 1 for i in range(32)}

                def v_te(t, masm=masm, mvb=mvb, vmm=vmm, vev=vev):
                    t.wait_ge(ctx.dv.h, masm["dv"])
                    for tt in range(NTT):
                        for os in range(2):
                            i = tt * 2 + os
                            if i >= 2:
                                t.wait_ge(ctx.dv.h, vev[i - 2])
                            for dc in range(NDC):
                                mm = t.matmul(
                                    out=pv[:, (i % 2) * 512:(i % 2 + 1) * 512],
                                    lhsT=xT[:, dc * T + tt * 128:dc * T + (tt + 1) * 128],
                                    rhs=WTx[:, dc * D + os * 512:dc * D + (os + 1) * 512],
                                    start=(dc == 0), stop=(dc == NDC - 1))
                            got = ctx.pe.inc(mm)
                            assert got == vmm[i]
                ctx.sec("tensor", v_te)

                def v_ve(v, vmm=vmm, vev=vev):
                    for tt in range(NTT):
                        for os in range(2):
                            i = tt * 2 + os
                            v.wait_ge(ctx.pe.h, vmm[i])
                            got = ctx.dv.inc(v.tensor_tensor(
                                out=Vr[:, tt, os * 8:(os + 1) * 8, 0:64],
                                in0=pv_r[:, i % 2, :, :],
                                in1=vb_r[:, os, :, :], op=ALU.add))
                            assert got == vev[i]
                ctx.sec("vector", v_ve)

        if "qT" in taps:
            _tap(ctx, tap_d["qT"], qT[:])
        if "kT" in taps:
            _tap(ctx, tap_d["kT"], kT[:])
        if "Vaug" in taps:
            _tap(ctx, tap_d["Vaug"], Vaug[:])
        if "WTo" in taps:
            _tap(ctx, tap_d["WTo"], WT_o[:])


def build_attn(ctx, E, stT, n_heads):
    taps, tap_d = E["taps"], E["tap_d"]
    qT, kT, Vaug, attnoutT = E["qT"], E["kT"], E["Vaug"], E["attnoutT"]
    cb128, ones128 = E["cb128"], E["ones128"]

    PT = sb(ctx, "PT", [128, 8 * T], BF16, stT)
    rrow = sb(ctx, "rrow", [128, 512], F32, stT)
    rbc = sb(ctx, "rbc", [128, 512], F32, stT)
    pS = psum(ctx, "pS", [128, T], F32, stT)
    pPV = psum(ctx, "pPV", [128, 4 * 512], F32, stT)

    m0 = ctx.marks()
    Vr = Vaug.ap().rearrange("p (tt hc) -> p tt hc", tt=NTT)

    smm, expm, pvm, bcm, rrm, evm = {}, {}, {}, {}, {}, {}
    pe_n, sc_n, dv_n = ctx.pe.n, ctx.sc.n, ctx.dv.n
    for h in range(n_heads):
        for ha in range(2):
            for kc8 in range(8):
                pe_n += 1
                smm[(h, ha * 8 + kc8)] = pe_n
                sc_n += 1
                expm[(h, ha * 8 + kc8)] = sc_n
            pe_n += 1
            pvm[(h, ha)] = pe_n
        for qb in range(4):
            dv_n += 1
            rrm[(h, qb)] = dv_n
            pe_n += 1
            bcm[(h, qb)] = pe_n
            dv_n += 1
            evm[(h, qb)] = dv_n
        evm[h] = dv_n

    def a_te(t):
        for h in range(n_heads):
            odd = h % 2
            p0 = 64 * odd
            bq = (h // 2) * T
            for ha in range(2):
                for kc8 in range(8):
                    kc = ha * 8 + kc8
                    if kc > 0:
                        t.wait_ge(ctx.sc.h, expm[(h, kc - 1)])
                    elif h > 0:
                        t.wait_ge(ctx.sc.h, expm[(h - 1, 15)])
                        t.wait_ge(ctx.dv.h, evm[h - 1])
                    else:
                        ctx.wait_marks(t, m0, ("dv", "sc"))
                    for qs in range(4):
                        mm = t.matmul(
                            out=pS[:, qs * 512:(qs + 1) * 512],
                            lhsT=kT[p0:p0 + 64, bq + kc * 128:bq + (kc + 1) * 128],
                            rhs=qT[p0:p0 + 64, bq + qs * 512:bq + (qs + 1) * 512],
                            start=True, stop=True)
                    got = ctx.pe.inc(mm)
                    assert got == smm[(h, kc)]
                t.wait_ge(ctx.sc.h, expm[(h, ha * 8 + 7)])
                for qb in range(4):
                    for kc8 in range(8):
                        kc = ha * 8 + kc8
                        rhsPT = PT[:, kc8 * T + qb * 512:kc8 * T + (qb + 1) * 512]
                        if not odd:
                            mm = t.matmul(
                                out=pPV[0:65, qb * 512:(qb + 1) * 512],
                                lhsT=Vr[:, kc, h * 65:(h + 1) * 65],
                                rhs=rhsPT, start=(kc == 0), stop=(kc == 15))
                        else:
                            t.matmul(
                                out=pPV[64:128, qb * 512:(qb + 1) * 512],
                                lhsT=Vr[:, kc, h * 65:h * 65 + 64],
                                rhs=rhsPT, start=(kc == 0), stop=(kc == 15),
                                tile_position=(0, 64))
                            mm = t.matmul(
                                out=pPV[0:1, qb * 512:(qb + 1) * 512],
                                lhsT=Vr[:, kc, h * 65 + 64:h * 65 + 65],
                                rhs=rhsPT, start=(kc == 0), stop=(kc == 15),
                                tile_position=(0, 0))
                got = ctx.pe.inc(mm)
                assert got == pvm[(h, ha)]
            for qb in range(4):
                t.wait_ge(ctx.dv.h, rrm[(h, qb)])
                if not odd:
                    mm = t.matmul(out=pS[0:64, qb * 512:(qb + 1) * 512],
                                  lhsT=ones128[64:65, 0:64],
                                  rhs=rrow[64:65, :],
                                  start=True, stop=True)
                else:
                    mm = t.matmul(out=pS[64:128, qb * 512:(qb + 1) * 512],
                                  lhsT=ones128[0:1, 0:64],
                                  rhs=rrow[0:1, :],
                                  start=True, stop=True,
                                  tile_position=(0, 64))
                got = ctx.pe.inc(mm)
                assert got == bcm[(h, qb)]
    ctx.sec("tensor", a_te)

    def a_sc(s):
        for h in range(n_heads):
            for ha in range(2):
                for kc8 in range(8):
                    kc = ha * 8 + kc8
                    s.wait_ge(ctx.pe.h, smm[(h, kc)])
                    if ha == 1 and kc8 == 0:
                        s.wait_ge(ctx.pe.h, pvm[(h, 0)])
                    a = s.activation(out=PT[:, kc8 * T:(kc8 + 1) * T], in_=pS[:],
                                     func=AF.Exp, scale=0.125, bias=cb128[:])
                    got = ctx.sc.inc(a)
                    assert got == expm[(h, kc)]
    ctx.sec("scalar", a_sc)

    def a_ve(v):
        for h in range(n_heads):
            odd = h % 2
            p0 = 64 * odd
            dcq = h // 2
            srow = 0 if odd else 64
            v.wait_ge(ctx.pe.h, pvm[(h, 1)])
            for qb in range(4):
                if qb > 0:
                    v.wait_ge(ctx.pe.h, bcm[(h, qb - 1)])
                r = v.reciprocal(out=rrow[srow:srow + 1, :],
                                 in_=pPV[srow:srow + 1, qb * 512:(qb + 1) * 512])
                got = ctx.dv.inc(r)
                assert got == rrm[(h, qb)]
                v.wait_ge(ctx.pe.h, bcm[(h, qb)])
                v.tensor_copy(out=rbc[p0:p0 + 64, :],
                              in_=pS[p0:p0 + 64, qb * 512:(qb + 1) * 512])
                ev = v.tensor_tensor(
                    out=attnoutT[p0:p0 + 64, dcq * T + qb * 512:dcq * T + (qb + 1) * 512],
                    in0=pPV[p0:p0 + 64, qb * 512:(qb + 1) * 512],
                    in1=rbc[p0:p0 + 64, :], op=ALU.mult)
                got = ctx.dv.inc(ev)
                assert got == evm[(h, qb)]
    ctx.sec("vector", a_ve)

    if "PT0" in taps:
        _tap(ctx, tap_d["PT0"], PT[:])
    if "attnoutT" in taps:
        _tap(ctx, tap_d["attnoutT"], attnoutT[:])


def build_ffn(ctx, E, stF):
    taps, tap_d = E["taps"], E["tap_d"]
    attnoutT, WT_o = E["attnoutT"], E["WT_o"]
    fbrows, ones128, epsln, epsqn = E["fbrows"], E["ones128"], E["epsln"], E["epsqn"]
    identb = E["identb"]
    wsc128, dqa, sqa, dqg, dqu, dqd = (E["wsc128"], E["dqa"], E["sqa"],
                                       E["dqg"], E["dqu"], E["dqd"])
    tg_d, tu_d, td_d, h_d, x1_d, y_d = (E["tg_d"], E["tu_d"], E["td_d"],
                                        E["h_d"], E["x1_d"], E["y_d"])
    x_d = E["x_d"]
    tern_st = E["tern_st_mark"]

    bObc = sb(ctx, "bObc", [128, D], F32, stF)
    xr = sb(ctx, "xr", [128, D], F32, stF)
    r1 = sb(ctx, "r1", [128, D], F32, stF)
    xpb = sb(ctx, "xpb", [128, D], F32, stF)
    lnt = sb(ctx, "lnt", [128, 8], F32, stF)
    xqb = sb(ctx, "xqb", [128, D], BF16, stF)
    xqTg = sb(ctx, "xqTg", [128, NDC * 512], BF16, stF)
    gss = sb(ctx, "gss", [128, 512], F32, stF)
    uss = sb(ctx, "uss", [128, 512], F32, stF)
    hst = sb(ctx, "hst", [128, 2 * 512], F32, stF)
    wgs = sb(ctx, "wgs", [128, 2 * NDC * 512], BF16, stF)
    wus = sb(ctx, "wus", [128, 2 * NDC * 512], BF16, stF)
    wds = sb(ctx, "wds", [128, 8 * D], BF16, stF)
    hrl = sb(ctx, "hrl", [128, INNER], F32, stF)
    hqb = sb(ctx, "hqb", [128, 2 * INNER], BF16, stF)
    hqT8 = sb(ctx, "hqT8", [128, 2 * 1024], BF16, stF)
    x1r = sb(ctx, "x1r", [128, D], F32, stF)
    yt = sb(ctx, "yt", [128, D], F32, stF)
    pO = psum(ctx, "pO", [128, 512], F32, stF)
    pG = psum(ctx, "pG", [128, 512], F32, stF)
    pU = psum(ctx, "pU", [128, 512], F32, stF)
    pD = psum(ctx, "pD", [128, 4 * 512], F32, stF)
    pXQ = psum(ctx, "pXQ", [128, 1024], BF16, stF)

    mo = ctx.marks()

    def ob_te(t, mo=mo):
        ctx.wait_marks(t, mo, ("dv",))
        for os in range(2):
            if os == 1:
                t.wait_ge(ctx.dv.h, mo["dv"] + 1)
            ctx.pe.inc(t.matmul(out=pO[:], lhsT=ones128[0:1, :],
                                rhs=fbrows[:, 3 * D + os * 512:3 * D + (os + 1) * 512],
                                start=True, stop=True))
    ctx.sec("tensor", ob_te)

    def ob_ve(v, mo=mo):
        for os in range(2):
            v.wait_ge(ctx.pe.h, mo["pe"] + os + 1)
            ctx.dv.inc(v.tensor_copy(out=bObc[:, os * 512:(os + 1) * 512], in_=pO[:]))
    ctx.sec("vector", ob_ve)
    ctx.barrier(on=("pe", "dv"))

    prev_tile_dv = ctx.dv.n
    prev_xqev_dv = ctx.dv.n
    prev_pair_tp_pe = 0
    prev_pair_mm_pe = 0

    for g in range(4):
        for k4 in range(4):
            tt = g * 4 + k4
            mt = ctx.marks()
            o_mm = {os: mt["pe"] + os + 1 for os in range(2)}
            xpb_dv = mt["dv"] + 1
            r1_dv = {0: mt["dv"] + 2, 1: mt["dv"] + 3}
            stat_dv = mt["dv"] + 4
            sqrt_sc = mt["sc"] + 1
            quant_dv = mt["dv"] + 5
            xq_ld = mt["ld"] + 16
            xpose_pe = mt["pe"] + 3
            xqev_dv = mt["dv"] + 6
            x1st_st = mt["st"] + 16

            def f_ld(sy, tt=tt, mt=mt):
                sy.wait_ge(ctx.dv.h, mt["dv"])
                ctx.ld.inc(sy.dma_start(out=xr[:], in_=x_d[tt * 128:(tt + 1) * 128, :]), 16)
            ctx.sec("sync", f_ld)

            def f_omm(t, tt=tt, mt=mt, o_mm=o_mm):
                t.wait_ge(ctx.dv.h, mt["dv"])
                for os in range(2):
                    if os == 1:
                        t.wait_ge(ctx.dv.h, r1_dv[0])
                    for dc in range(NDC):
                        mm = t.matmul(
                            out=pO[:],
                            lhsT=attnoutT[:, dc * T + tt * 128:dc * T + (tt + 1) * 128],
                            rhs=WT_o[:, dc * D + os * 512:dc * D + (os + 1) * 512],
                            start=(dc == 0), stop=(dc == NDC - 1))
                    got = ctx.pe.inc(mm)
                    assert got == o_mm[os]
            ctx.sec("tensor", f_omm)

            def f_ln1(v, tt=tt, mt=mt, o_mm=o_mm):
                v.wait_ge(ctx.ld.h, xq_ld)
                got = ctx.dv.inc(v.tensor_add(out=xpb[:], in0=xr[:], in1=bObc[:]))
                assert got == xpb_dv
                for os in range(2):
                    v.wait_ge(ctx.pe.h, o_mm[os])
                    got = ctx.dv.inc(v.tensor_add(
                        out=r1[:, os * 512:(os + 1) * 512], in0=pO[:],
                        in1=xpb[:, os * 512:(os + 1) * 512]))
                    assert got == r1_dv[os]
                v.reduce_sum(out=lnt[:, 0:1], in_=r1[:], axis=AX.X)
                v.tensor_scalar_mul(out=lnt[:, 0:1], in0=lnt[:, 0:1], scalar1=1.0 / D)
                v.scalar_tensor_tensor(out=hrl[:, 0:D], in0=r1[:], scalar=1.0,
                                       in1=r1[:], op0=ALU.mult, op1=ALU.mult,
                                       accum_out=lnt[:, 1:2])
                v.tensor_scalar_mul(out=lnt[:, 1:2], in0=lnt[:, 1:2], scalar1=1.0 / D)
                v.tensor_mul(out=lnt[:, 2:3], in0=lnt[:, 0:1], in1=lnt[:, 0:1])
                got = ctx.dv.inc(v.tensor_sub(out=lnt[:, 1:2], in0=lnt[:, 1:2],
                                              in1=lnt[:, 2:3]))
                assert got == stat_dv
            ctx.sec("vector", f_ln1)

            def f_sq(s, mt=mt):
                s.wait_ge(ctx.dv.h, stat_dv)
                got = ctx.sc.inc(s.activation(out=lnt[:, 3:4], in_=lnt[:, 1:2],
                                              func=AF.Sqrt, bias=epsln[:]))
                assert got == sqrt_sc
            ctx.sec("scalar", f_sq)

            def f_q(v, tt=tt):
                v.wait_ge(ctx.sc.h, sqrt_sc)
                v.reciprocal(out=lnt[:, 4:5], in_=lnt[:, 3:4])
                v.tensor_scalar(out=r1[:], in0=r1[:], scalar1=lnt[:, 0:1],
                                scalar2=lnt[:, 4:5], op0=ALU.subtract, op1=ALU.mult)
                v.reduce_max(out=lnt[:, 5:6], in_=r1[:], axis=AX.X,
                             apply_absolute_value=True)
                v.tensor_scalar_max(out=lnt[:, 5:6], in0=lnt[:, 5:6], scalar1=1e-5)
                v.tensor_scalar_mul(out=dqa[:, tt:tt + 1], in0=lnt[:, 5:6],
                                    scalar1=1.0 / 127.0)
                v.reciprocal(out=lnt[:, 6:7], in_=lnt[:, 5:6])
                v.tensor_scalar_mul(out=sqa[:, tt:tt + 1], in0=lnt[:, 6:7], scalar1=127.0)
                v.tensor_mul(out=dqg[:, tt:tt + 1], in0=dqa[:, tt:tt + 1], in1=wsc128[:, 0:1])
                v.tensor_mul(out=dqu[:, tt:tt + 1], in0=dqa[:, tt:tt + 1], in1=wsc128[:, 1:2])
                v.tensor_scalar(out=xpb[:], in0=r1[:], scalar1=sqa[:, tt:tt + 1],
                                scalar2=MAGIC, op0=ALU.mult, op1=ALU.add)
                v.tensor_single_scalar(out=xpb[:], in_=xpb[:], scalar=MAGIC,
                                       op=ALU.subtract)
                v.tensor_copy(out=xpb.bitcast(I32)[:], in_=xpb.bitcast(I32)[:])
                got = ctx.dv.inc(v.tensor_copy(out=xqb[:], in_=xpb[:]))
                assert got == quant_dv
            ctx.sec("vector", f_q)

            def f_x1st(sy, tt=tt):
                sy.wait_ge(ctx.dv.h, quant_dv)
                got = ctx.st.inc(sy.dma_start(out=x1_d[tt * 128:(tt + 1) * 128, :],
                                              in_=r1[:]), 16)
                assert got == x1st_st
            ctx.sec("sync", f_x1st)

            def f_xqt(t, k4=k4, pxf=prev_xqev_dv):
                t.wait_ge(ctx.dv.h, quant_dv)
                t.wait_ge(ctx.dv.h, pxf)
                for dc in range(NDC):
                    mm = t.transpose(out=pXQ[:, dc * 128:(dc + 1) * 128],
                                     in_=xqb[:, dc * 128:(dc + 1) * 128],
                                     identity=identb[:])
                got = ctx.pe.inc(mm)
                assert got == xpose_pe
            ctx.sec("tensor", f_xqt)

            def f_xqe(v, k4=k4):
                v.wait_ge(ctx.pe.h, xpose_pe)
                xqr = xqTg.ap().rearrange("p (dc t) -> p dc t", dc=NDC)
                pxr = pXQ.ap().rearrange("p (dc t) -> p dc t", dc=NDC)
                got = ctx.dv.inc(v.tensor_copy(out=xqr[:, :, k4 * 128:(k4 + 1) * 128],
                                               in_=pxr[:]))
                assert got == xqev_dv
            ctx.sec("vector", f_xqe)
            prev_xqev_dv = xqev_dv
            prev_tile_dv = ctx.dv.n

        mg = ctx.marks()
        gu_mm = {}
        pe_n = mg["pe"]
        for is_ in range(8):
            for k4 in range(4):
                pe_n += 2
                gu_mm[(is_, k4)] = pe_n
        gu_sc = {}
        sc_n = mg["sc"]
        for is_ in range(8):
            for k4 in range(4):
                sc_n += 1
                gu_sc[(is_, k4)] = sc_n
        gu_ev = {}
        dv_n = mg["dv"]
        for is_ in range(8):
            for k4 in range(4):
                dv_n += 2
                gu_ev[(is_, k4)] = dv_n
        hst_mark = {}
        st_n = mg["st"]
        for i in range(32):
            st_n += 16
            hst_mark[i] = st_n
        h_all_stored = st_n
        lw_is = {is_: mg["lw"] + 32 * (is_ + 1) for is_ in range(8)}

        def gu_sy(sy, g=g, mg=mg):
            for is_ in range(8):
                if is_ >= 2:
                    sy.wait_ge(ctx.pe.h, gu_mm[(is_ - 2, 3)])
                b = is_ % 2
                if g == 0 and is_ == 0:
                    sy.wait_ge(ctx.st.h, tern_st)
                ctx.lw.inc(sy.dma_start(
                    out=wgs[:, b * 4096:(b + 1) * 4096].rearrange(
                        "p (ic o) -> p ic o", ic=NDC),
                    in_=tg_d[:, is_ * 512:(is_ + 1) * 512].rearrange(
                        "(ic p) o -> p ic o", p=128)), 16)
                ctx.lw.inc(sy.dma_start(
                    out=wus[:, b * 4096:(b + 1) * 4096].rearrange(
                        "p (ic o) -> p ic o", ic=NDC),
                    in_=tu_d[:, is_ * 512:(is_ + 1) * 512].rearrange(
                        "(ic p) o -> p ic o", p=128)), 16)
                if is_ >= 1:
                    for k4 in range(4):
                        i = (is_ - 1) * 4 + k4
                        sy.wait_ge(ctx.dv.h, gu_ev[(is_ - 1, k4)])
                        got = ctx.st.inc(sy.dma_start(
                            out=h_d[(g * 4 + k4) * 128:(g * 4 + k4 + 1) * 128,
                                    (is_ - 1) * 512:is_ * 512],
                            in_=hst[:, (i % 2) * 512:(i % 2 + 1) * 512]), 16)
                        assert got == hst_mark[i]
            for k4 in range(4):
                i = 7 * 4 + k4
                sy.wait_ge(ctx.dv.h, gu_ev[(7, k4)])
                got = ctx.st.inc(sy.dma_start(
                    out=h_d[(g * 4 + k4) * 128:(g * 4 + k4 + 1) * 128, 7 * 512:8 * 512],
                    in_=hst[:, (i % 2) * 512:(i % 2 + 1) * 512]), 16)
                assert got == hst_mark[i]
        ctx.sec("sync", gu_sy)

        def gu_te(t, g=g, mg=mg):
            for is_ in range(8):
                b = is_ % 2
                t.wait_ge(ctx.lw.h, lw_is[is_])
                for k4 in range(4):
                    if (is_, k4) != (0, 0):
                        prev = (is_, k4 - 1) if k4 > 0 else (is_ - 1, 3)
                        t.wait_ge(ctx.dv.h, gu_ev[prev])
                    else:
                        t.wait_ge(ctx.dv.h, mg["dv"])
                    for ic in range(NDC):
                        t.matmul(out=pG[:],
                                 lhsT=xqTg[:, ic * 512 + k4 * 128:ic * 512 + (k4 + 1) * 128],
                                 rhs=wgs[:, b * 4096 + ic * 512:b * 4096 + (ic + 1) * 512],
                                 start=(ic == 0), stop=(ic == NDC - 1))
                    for ic in range(NDC):
                        mm = t.matmul(out=pU[:],
                                      lhsT=xqTg[:, ic * 512 + k4 * 128:ic * 512 + (k4 + 1) * 128],
                                      rhs=wus[:, b * 4096 + ic * 512:b * 4096 + (ic + 1) * 512],
                                      start=(ic == 0), stop=(ic == NDC - 1))
                    got = ctx.pe.inc(mm, 2)
                    assert got == gu_mm[(is_, k4)]
        ctx.sec("tensor", gu_te)

        def gu_scf(s, g=g):
            for is_ in range(8):
                for k4 in range(4):
                    tt = g * 4 + k4
                    s.wait_ge(ctx.pe.h, gu_mm[(is_, k4)])
                    if (is_, k4) != (0, 0):
                        prev = (is_, k4 - 1) if k4 > 0 else (is_ - 1, 3)
                        s.wait_ge(ctx.dv.h, gu_ev[prev])
                    a = s.activation(out=gss[:], in_=pG[:], func=AF.Silu,
                                     scale=dqg[:, tt:tt + 1])
                    got = ctx.sc.inc(a)
                    assert got == gu_sc[(is_, k4)]
        ctx.sec("scalar", gu_scf)

        def gu_ve(v, g=g):
            for is_ in range(8):
                for k4 in range(4):
                    tt = g * 4 + k4
                    i = is_ * 4 + k4
                    v.wait_ge(ctx.pe.h, gu_mm[(is_, k4)])
                    v.tensor_scalar_mul(out=uss[:], in0=pU[:], scalar1=dqu[:, tt:tt + 1])
                    v.wait_ge(ctx.sc.h, gu_sc[(is_, k4)])
                    if i >= 2:
                        v.wait_ge(ctx.st.h, hst_mark[i - 1])
                    ev = v.tensor_mul(out=hst[:, (i % 2) * 512:(i % 2 + 1) * 512],
                                      in0=gss[:], in1=uss[:])
                    got = ctx.dv.inc(ev, 2)
                    assert got == gu_ev[(is_, k4)]
        ctx.sec("vector", gu_ve)

        for pr_ in range(2):
            tts = [g * 4 + pr_ * 2, g * 4 + pr_ * 2 + 1]
            hq_dv = {}
            for tk, tt in enumerate(tts):
                mh = ctx.marks()
                hld = mh["ld"] + 16

                def h_ld(sy, tt=tt, mh=mh, pp=prev_pair_tp_pe):
                    sy.wait_ge(ctx.dv.h, mh["dv"])
                    if pp:
                        sy.wait_ge(ctx.pe.h, pp)
                    sy.wait_ge(ctx.st.h, h_all_stored)
                    ctx.ld.inc(sy.dma_start(out=hrl[:], in_=h_d[tt * 128:(tt + 1) * 128, :]), 16)
                ctx.sec("sync", h_ld)

                def h_q(v, tt=tt, tk=tk, hld=hld):
                    v.wait_ge(ctx.ld.h, hld)
                    v.reduce_max(out=lnt[:, 5:6], in_=hrl[:], axis=AX.X,
                                 apply_absolute_value=True)
                    v.tensor_scalar_max(out=lnt[:, 5:6], in0=lnt[:, 5:6], scalar1=1e-5)
                    v.tensor_scalar_mul(out=lnt[:, 7:8], in0=lnt[:, 5:6], scalar1=1.0 / 127.0)
                    v.tensor_mul(out=dqd[:, tt:tt + 1], in0=lnt[:, 7:8], in1=wsc128[:, 2:3])
                    v.reciprocal(out=lnt[:, 6:7], in_=lnt[:, 5:6])
                    v.tensor_scalar_mul(out=lnt[:, 6:7], in0=lnt[:, 6:7], scalar1=127.0)
                    v.tensor_scalar(out=hrl[:], in0=hrl[:], scalar1=lnt[:, 6:7],
                                    scalar2=MAGIC, op0=ALU.mult, op1=ALU.add)
                    v.tensor_single_scalar(out=hrl[:], in_=hrl[:], scalar=MAGIC,
                                           op=ALU.subtract)
                    v.tensor_copy(out=hrl.bitcast(I32)[:], in_=hrl.bitcast(I32)[:])
                    hq_dv[tk] = ctx.dv.inc(v.tensor_copy(
                        out=hqb[:, tk * INNER:(tk + 1) * INNER], in_=hrl[:]))
                ctx.sec("vector", h_q)

            mp = ctx.marks()
            dn_lw = {icg: mp["lw"] + 16 * (icg + 1) for icg in range(4)}
            dn_tp, dn_mm, dn_hq = {}, {}, {}
            pe_n = mp["pe"]
            dv_n = mp["dv"]
            for icg in range(4):
                for tk in range(2):
                    pe_n += 1
                    dn_tp[(icg, tk)] = pe_n
                    dv_n += 1
                    dn_hq[(icg, tk)] = dv_n
                    pe_n += 1
                    dn_mm[(icg, tk)] = pe_n

            def dn_sy(sy, mp=mp):
                for icg in range(4):
                    if icg >= 1:
                        sy.wait_ge(ctx.pe.h, dn_mm[(icg - 1, 1)])
                    ctx.lw.inc(sy.dma_start(
                        out=wds.ap().rearrange("p (ic o) -> p ic o", ic=8),
                        in_=td_d[icg * 1024:(icg + 1) * 1024, :].rearrange(
                            "(ic p) o -> p ic o", p=128)), 16)
            ctx.sec("sync", dn_sy)

            def dn_te(t, mp=mp):
                for icg in range(4):
                    t.wait_ge(ctx.lw.h, dn_lw[icg])
                    for tk in range(2):
                        if (icg, tk) == (0, 0):
                            t.wait_ge(ctx.dv.h, hq_dv[1])
                        if (icg, tk) != (0, 0):
                            prevk = (icg, tk - 1) if tk > 0 else (icg - 1, 1)
                            t.wait_ge(ctx.dv.h, dn_hq[prevk])
                        else:
                            t.wait_ge(ctx.dv.h, prev_xqev_dv)
                        for i8 in range(8):
                            ic = icg * 8 + i8
                            mm = t.transpose(
                                out=pXQ[:, i8 * 128:(i8 + 1) * 128],
                                in_=hqb[:, tk * INNER + ic * 128:tk * INNER + (ic + 1) * 128],
                                identity=identb[:])
                        got = ctx.pe.inc(mm)
                        assert got == dn_tp[(icg, tk)]
                        t.wait_ge(ctx.dv.h, dn_hq[(icg, tk)])
                        for i8 in range(8):
                            for os in range(2):
                                mm = t.matmul(
                                    out=pD[:, (tk * 2 + os) * 512:(tk * 2 + os + 1) * 512],
                                    lhsT=hqT8[:, tk * 1024 + i8 * 128:tk * 1024 + (i8 + 1) * 128],
                                    rhs=wds[:, i8 * D + os * 512:i8 * D + (os + 1) * 512],
                                    start=(icg == 0 and i8 == 0),
                                    stop=(icg == 3 and i8 == 7))
                        got = ctx.pe.inc(mm)
                        assert got == dn_mm[(icg, tk)]
            ctx.sec("tensor", dn_te)

            def dn_ve(v, mp=mp):
                for icg in range(4):
                    for tk in range(2):
                        v.wait_ge(ctx.pe.h, dn_tp[(icg, tk)])
                        got = ctx.dv.inc(v.tensor_copy(
                            out=hqT8[:, tk * 1024:(tk + 1) * 1024], in_=pXQ[:]))
                        assert got == dn_hq[(icg, tk)]
            ctx.sec("vector", dn_ve)
            prev_pair_tp_pe = dn_tp[(3, 1)]
            prev_pair_mm_pe = dn_mm[(3, 1)]

            for tk, tt in enumerate(tts):
                me = ctx.marks()
                x1_ld = me["ld"] + 16
                ev_dv = me["dv"] + 1
                st2_dv = me["dv"] + 2
                sq2_sc = me["sc"] + 1
                y_dv = me["dv"] + 3
                y_st = me["st"] + 16

                def e_ld(sy, tt=tt, me=me):
                    sy.wait_ge(ctx.dv.h, me["dv"])
                    sy.wait_ge(ctx.st.h, me["st"])
                    ctx.ld.inc(sy.dma_start(out=x1r[:], in_=x1_d[tt * 128:(tt + 1) * 128, :]), 16)
                ctx.sec("sync", e_ld)

                def e_ev(v, tt=tt, tk=tk, me=me):
                    v.wait_ge(ctx.ld.h, x1_ld)
                    v.wait_ge(ctx.pe.h, dn_mm[(3, 1)])
                    for os in range(2):
                        ev = v.scalar_tensor_tensor(
                            out=r1[:, os * 512:(os + 1) * 512],
                            in0=pD[:, (tk * 2 + os) * 512:(tk * 2 + os + 1) * 512],
                            scalar=dqd[:, tt:tt + 1],
                            in1=x1r[:, os * 512:(os + 1) * 512],
                            op0=ALU.mult, op1=ALU.add)
                    got = ctx.dv.inc(ev)
                    assert got == ev_dv
                    v.reduce_sum(out=lnt[:, 0:1], in_=r1[:], axis=AX.X)
                    v.tensor_scalar_mul(out=lnt[:, 0:1], in0=lnt[:, 0:1], scalar1=1.0 / D)
                    v.scalar_tensor_tensor(out=hrl[:, 0:D], in0=r1[:], scalar=1.0,
                                           in1=r1[:], op0=ALU.mult, op1=ALU.mult,
                                           accum_out=lnt[:, 1:2])
                    v.tensor_scalar_mul(out=lnt[:, 1:2], in0=lnt[:, 1:2], scalar1=1.0 / D)
                    v.tensor_mul(out=lnt[:, 2:3], in0=lnt[:, 0:1], in1=lnt[:, 0:1])
                    got = ctx.dv.inc(v.tensor_sub(out=lnt[:, 1:2], in0=lnt[:, 1:2],
                                                  in1=lnt[:, 2:3]))
                    assert got == st2_dv
                ctx.sec("vector", e_ev)

                def e_sq(s, me=me):
                    s.wait_ge(ctx.dv.h, st2_dv)
                    got = ctx.sc.inc(s.activation(out=lnt[:, 3:4], in_=lnt[:, 1:2],
                                                  func=AF.Sqrt, bias=epsln[:]))
                    assert got == sq2_sc
                ctx.sec("scalar", e_sq)

                def e_out(v, me=me):
                    v.wait_ge(ctx.sc.h, sq2_sc)
                    v.reciprocal(out=lnt[:, 4:5], in_=lnt[:, 3:4])
                    got = ctx.dv.inc(v.tensor_scalar(out=yt[:], in0=r1[:],
                                                     scalar1=lnt[:, 0:1],
                                                     scalar2=lnt[:, 4:5],
                                                     op0=ALU.subtract, op1=ALU.mult))
                    assert got == y_dv
                ctx.sec("vector", e_out)

                def e_st(sy, tt=tt, me=me):
                    sy.wait_ge(ctx.dv.h, y_dv)
                    got = ctx.st.inc(sy.dma_start(out=y_d[tt * 128:(tt + 1) * 128, :],
                                                  in_=yt[:]), 16)
                    assert got == y_st
                ctx.sec("sync", e_st)

    if "x1" in taps:
        def fx(sy):
            m = ctx.marks()
            ctx.wait_marks(sy, m, ("pe", "dv", "sc", "gp", "ld", "lw"))
            sy.wait_ge(ctx.st.h, m["st"])
            ctx.st.inc(sy.dma_start(out=tap_d["x1"], in_=x1_d), 16)
        ctx.sec("sync", fx)
    if "hsc" in taps:
        def fh(sy):
            m = ctx.marks()
            ctx.wait_marks(sy, m, ("pe", "dv", "sc", "gp", "ld", "lw"))
            sy.wait_ge(ctx.st.h, m["st"])
            ctx.st.inc(sy.dma_start(out=tap_d["hsc"], in_=h_d), 16)
        ctx.sec("sync", fh)


def make_inputs(L0_attn_x, L0_attn_qW, L0_attn_qb, L0_attn_kW, L0_attn_kb,
                L0_attn_vW, L0_attn_vb, L0_attn_oW, L0_attn_ob, L0_attn_J,
                L0_ffn_w_gate, L0_ffn_w_up, L0_ffn_w_down):
    import ml_dtypes
    common = {
        "qW": np.ascontiguousarray(np.asarray(L0_attn_qW, dtype=np.float32)),
        "kW": np.ascontiguousarray(np.asarray(L0_attn_kW, dtype=np.float32)),
        "vW": np.ascontiguousarray(np.asarray(L0_attn_vW, dtype=np.float32)),
        "oW": np.ascontiguousarray(np.asarray(L0_attn_oW, dtype=np.float32)),
        "qb": np.ascontiguousarray(np.asarray(L0_attn_qb, dtype=np.float32)),
        "kb": np.ascontiguousarray(np.asarray(L0_attn_kb, dtype=np.float32)),
        "vb": np.ascontiguousarray(np.asarray(L0_attn_vb, dtype=np.float32)),
        "ob": np.ascontiguousarray(np.asarray(L0_attn_ob, dtype=np.float32)),
        "J": np.ascontiguousarray(np.asarray(L0_attn_J, dtype=np.float32)),
        "wgT": np.ascontiguousarray(np.asarray(L0_ffn_w_gate, dtype=np.float32).T),
        "wuT": np.ascontiguousarray(np.asarray(L0_ffn_w_up, dtype=np.float32).T),
        "wdT": np.ascontiguousarray(np.asarray(L0_ffn_w_down, dtype=np.float32).T),
        "idb": np.eye(128, dtype=ml_dtypes.bfloat16),
        "idf": np.eye(128, dtype=np.float32),
    }
    xs = np.asarray(L0_attn_x, dtype=np.float32)
    return [dict(common, x=np.ascontiguousarray(xs[i])) for i in range(xs.shape[0])]


def kernel(L0_attn_x, L0_attn_qW, L0_attn_qb, L0_attn_kW, L0_attn_kb,
           L0_attn_vW, L0_attn_vb, L0_attn_oW, L0_attn_ob, L0_attn_J,
           L0_ffn_w_gate, L0_ffn_w_up, L0_ffn_w_down):
    nc = build()
    in_maps = make_inputs(L0_attn_x, L0_attn_qW, L0_attn_qb, L0_attn_kW,
                          L0_attn_kb, L0_attn_vW, L0_attn_vb, L0_attn_oW,
                          L0_attn_ob, L0_attn_J, L0_ffn_w_gate, L0_ffn_w_up,
                          L0_ffn_w_down)
    res = run_bass_kernel_spmd(nc, in_maps, core_ids=list(range(len(in_maps))))
    return np.stack([r["y"] for r in res.results], axis=0)
